# version 02
# baseline (speedup 1.0000x reference)
import math
import numpy as np
import ml_dtypes
import concourse.bass as bass
import concourse.mybir as mybir
from concourse.bass_utils import run_bass_kernel_spmd

F32 = mybir.dt.float32
BF16 = mybir.dt.bfloat16
AF = mybir.ActivationFunctionType
ALU = mybir.AluOpType
AX = mybir.AxisListType

B, T, D = 4, 4096, 2048
NH, HD = 16, 128
NDC = D // 128
E, EI, NSTR = 16, 1408, 11
SI, SSTR = 2816, 22
NT = 16
GT = 1024
NEG = -1e30
EPS = 1e-5
QSCALE = 1.0 / math.sqrt(128.0)
SLOPE0 = 0.5 ** (8.0 / 16.0)
SLOPER = 0.5 ** (8.0 / 16.0)


def build(debug=False):
    nc = bass.Bass("TRN2", target_bir_lowering=False)

    xin = nc.dram_tensor("xb", [T, D], F32, kind="ExternalInput")
    wqt = nc.dram_tensor("wqt", [D, D], F32, kind="ExternalInput")
    wkt = nc.dram_tensor("wkt", [D, D], F32, kind="ExternalInput")
    wvt = nc.dram_tensor("wvt", [D, D], F32, kind="ExternalInput")
    wot = nc.dram_tensor("wot", [D, D], F32, kind="ExternalInput")
    gwt = nc.dram_tensor("gwt", [D, E], F32, kind="ExternalInput")
    w13 = nc.dram_tensor("w13", [E, NSTR, 2, D, 128], BF16, kind="ExternalInput")
    w2r = nc.dram_tensor("w2r", [E, EI, D], BF16, kind="ExternalInput")
    sw13 = nc.dram_tensor("sw13", [SSTR, 2, D, 128], BF16, kind="ExternalInput")
    sw2r = nc.dram_tensor("sw2r", [SI, D], BF16, kind="ExternalInput")
    iota = nc.dram_tensor("iota", [128, T], F32, kind="ExternalInput")
    maskd = nc.dram_tensor("maskd", [2, 128, 512], F32, kind="ExternalInput")
    idf = nc.dram_tensor("idf", [128, 128], F32, kind="ExternalInput")
    idb = nc.dram_tensor("idb", [128, 128], BF16, kind="ExternalInput")
    out = nc.dram_tensor("out", [NT * 128, D], F32, kind="ExternalOutput")

    skind = "ExternalOutput" if debug else "Internal"
    KT_s = nc.dram_tensor("KT_s", [NH, 128, T], BF16, kind=skind)
    V_s = nc.dram_tensor("V_s", [NH, T, 128], BF16, kind=skind)
    QT_s = nc.dram_tensor("QT_s", [NH, 128, NT * 128], BF16, kind=skind)
    x2_s = nc.dram_tensor("x2_s", [NT * 128, D], F32, kind=skind)
    x2T_s = nc.dram_tensor("x2T_s", [D, NT * 128], BF16, kind=skind)
    OT_s = nc.dram_tensor("OT_s", [NT, NH, 128, 128], BF16, kind=skind)

    Q = {"t": [], "v": [], "a": [], "g": [], "s": []}
    C = {k: 0 for k in ("st", "sv", "sa", "sp", "dg", "ds", "dx", "do")}

    def W(q, sem, val):
        if val:
            Q[q].append(("w", sem, val))

    def OP(q, fn, sem=None, n=1):
        if sem is None:
            Q[q].append(("i", fn, None, 0))
            return None
        C[sem] += n
        Q[q].append(("i", fn, sem, n))
        return C[sem]

    def CP(En, o, i):
        if hasattr(En, "tensor_copy"):
            return En.tensor_copy(o, i)
        return En.copy(o, i)

    def MM(out_ap, lhsT, rhs, start, stop, sem=None):
        return OP("t", lambda En, o=out_ap, l=lhsT, r=rhs, s1=start, s2=stop:
                  En.matmul(o, l, r, start=s1, stop=s2), sem)

    def TR(out_ap, in_ap, ident, sem=None):
        return OP("t", lambda En, o=out_ap, i=in_ap, d=ident: En.transpose(o, i, d), sem)

    def DMA(q, out_ap, in_ap):
        sem = {"g": "dg", "s": "ds", "a": "dx"}[q]
        C[sem] += 16
        Q[q].append(("i", lambda En, o=out_ap, i=in_ap: En.dma_start(o, i), sem, 16))
        return C[sem]

    def DMAo(out_ap, in_ap):
        C["do"] += 16
        Q["s"].append(("i", lambda En, o=out_ap, i=in_ap: En.dma_start(o, i), "do", 16))
        return C["do"]

    import contextlib
    outer = contextlib.ExitStack()
    idf_sb = outer.enter_context(nc.sbuf_tensor("idf_sb", [128, 128], F32))
    idb_sb = outer.enter_context(nc.sbuf_tensor("idb_sb", [128, 128], BF16))
    dw_sb = outer.enter_context(nc.sbuf_tensor("dw_sb", [128, NT * E], F32))
    small = outer.enter_context(nc.sbuf_tensor("small", [128, 64], F32))
    ot_stack = contextlib.ExitStack()
    mask_sb = ot_stack.enter_context(nc.sbuf_tensor("mask_sb", [128, 1024], F32))
    ot_stg = [ot_stack.enter_context(nc.sbuf_tensor(f"ot_stg{i}", [128, 256], BF16)) for i in range(2)]
    kstage = [ot_stack.enter_context(nc.sbuf_tensor(f"kstage{i}", [128, 2048], BF16)) for i in range(2)]
    vstage = [ot_stack.enter_context(nc.sbuf_tensor(f"vstage{i}", [128, 1024], BF16)) for i in range(2)]
    qstage = [ot_stack.enter_context(nc.sbuf_tensor(f"qstage{i}", [128, 1024], BF16)) for i in range(2)]
    OTld = [ot_stack.enter_context(nc.sbuf_tensor(f"OTld{i}", [128, NH * 128], BF16)) for i in range(4)]
    ot_stack.enter_context(nc.sbuf_tensor("otpad", [128, 17280], mybir.dt.uint8))

    ev_idf = DMA("g", idf_sb[:], idf[:])
    ev_idb = DMA("g", idb_sb[:], idb[:])
    OP("v", lambda En: En.memset(small[:, 10:11], EPS), "sv")

    p2 = contextlib.ExitStack()
    xT = p2.enter_context(nc.sbuf_tensor("xT", [128, NDC * 2048], BF16))
    xstage = [p2.enter_context(nc.sbuf_tensor(f"xstage{i}", [128, D], F32)) for i in range(2)]
    wslab = [p2.enter_context(nc.sbuf_tensor(f"wslab{i}", [128, NDC * 128], BF16)) for i in range(2)]
    wv_sb = [p2.enter_context(nc.sbuf_tensor("wv_sb0", [128, NDC * 1024], BF16))] * 2
    pA = p2.enter_context(nc.psum_tensor("pA", [128, 2048], F32))
    pB = p2.enter_context(nc.psum_tensor("pB", [128, 2048], F32))
    pAB = [pA, pB]

    xload_ev = [None, None]
    xfree_ev = [None, None]
    trevac_ev = [None, None]
    kv_evac_ev = [None, None]
    kv_store_ev = [None] * 6
    wslab_ev = [None, None]
    wslab_use_ev = [None, None]
    wv_ev = [None, None]
    wv_use_ev = [None, None]

    for H in range(2):
        for tt in range(16):
            sl = tt % 2
            W("a", "st", xfree_ev[sl])
            xload_ev[sl] = DMA("a", xstage[sl][:],
                               xin[H * 2048 + tt * 128: H * 2048 + (tt + 1) * 128, :])
            W("t", "dx", xload_ev[sl])
            if tt == 0 and H == 0:
                W("t", "dg", ev_idf)
            for g in range(4):
                rg = g % 2
                if trevac_ev[rg] is not None:
                    W("t", trevac_ev[rg][0], trevac_ev[rg][1])
                lastv = None
                for c4 in range(4):
                    c = g * 4 + c4
                    lastv = TR(pAB[rg][:, c4 * 128:(c4 + 1) * 128],
                               xstage[sl][:, c * 128:(c + 1) * 128], idf_sb[:],
                               sem="st" if c4 == 3 else None)
                if g == 3:
                    xfree_ev[sl] = lastv
                qe = "v" if (g % 2 == 0) else "a"
                se = "sv" if qe == "v" else "sa"
                W(qe, "st", lastv)
                dst = xT.ap().rearrange("p (dc c) -> p dc c", dc=NDC)[:, g * 4:(g + 1) * 4,
                                                                     tt * 128:(tt + 1) * 128]
                src = pAB[rg].ap()[:, 0:512].rearrange("p (q c) -> p q c", q=4)
                evv = OP(qe, lambda En, o=dst, i=src: CP(En, o, i), se)
                trevac_ev[rg] = (se, evv)
        for rg in range(2):
            if trevac_ev[rg] is not None:
                W("t", trevac_ev[rg][0], trevac_ev[rg][1])
                trevac_ev[rg] = None

        for hc in range(NH):
            sl = hc % 2
            if wslab_use_ev[sl] is not None:
                W("g", "st", wslab_use_ev[sl])
            wslab_ev[sl] = DMA("g", wslab[sl].ap().rearrange("p (dc c) -> p dc c", dc=NDC),
                               wkt.ap()[:, hc * 128:(hc + 1) * 128].rearrange("(dc p) c -> p dc c", p=128))
            rg = hc % 2
            if kv_evac_ev[rg] is not None:
                W("t", kv_evac_ev[rg][0], kv_evac_ev[rg][1])
            W("t", "dg", wslab_ev[sl])
            lastv = None
            for dc in range(NDC):
                for ncn in range(4):
                    lastv = MM(pAB[rg][:, ncn * 512:(ncn + 1) * 512],
                               wslab[sl][:, dc * 128:(dc + 1) * 128],
                               xT[:, dc * 2048 + ncn * 512: dc * 2048 + (ncn + 1) * 512],
                               start=(dc == 0), stop=(dc == NDC - 1),
                               sem="st" if (dc == NDC - 1 and ncn == 3) else None)
            wslab_use_ev[sl] = lastv
            qe = "v" if (hc % 2 == 0) else "a"
            se = "sv" if qe == "v" else "sa"
            ssl = hc % 2
            W(qe, "st", lastv)
            if kv_store_ev[ssl] is not None:
                W(qe, "dg", kv_store_ev[ssl])
            evv = OP(qe, lambda En, o=kstage[ssl][:], i=pAB[rg][:]: CP(En, o, i), se)
            kv_evac_ev[rg] = (se, evv)
            W("g", se, evv)
            kv_store_ev[ssl] = DMA("g", KT_s.ap()[hc, :, H * 2048:(H + 1) * 2048], kstage[ssl][:])

        xTv = xT.ap().rearrange("p (dc u par c) -> p dc u par c", dc=NDC, u=8, par=2)
        for hc in range(NH):
            sl = hc % 2
            if wslab_use_ev[sl] is not None:
                W("g", "st", wslab_use_ev[sl])
            wslab_ev[sl] = DMA("g", wslab[sl].ap().rearrange("p (dc c) -> p dc c", dc=NDC),
                               wqt.ap()[:, hc * 128:(hc + 1) * 128].rearrange("(dc p) c -> p dc c", p=128))
            rg = hc % 2
            if kv_evac_ev[rg] is not None:
                W("t", kv_evac_ev[rg][0], kv_evac_ev[rg][1])
            W("t", "dg", wslab_ev[sl])
            lastv = None
            for dc in range(NDC):
                for halfq in range(2):
                    o = pAB[rg].ap()[:, halfq * 512:(halfq + 1) * 512].rearrange("p (w c) -> p w c", w=4)
                    r = xTv[:, dc, halfq * 4:(halfq + 1) * 4, 0, :]
                    lastv = MM(o, wslab[sl][:, dc * 128:(dc + 1) * 128], r,
                               start=(dc == 0), stop=(dc == NDC - 1),
                               sem="st" if (dc == NDC - 1 and halfq == 1) else None)
            wslab_use_ev[sl] = lastv
            qe = "v" if (hc % 2 == 0) else "a"
            se = "sv" if qe == "v" else "sa"
            ssl = 2 + hc % 2
            W(qe, "st", lastv)
            if kv_store_ev[ssl] is not None:
                W(qe, "dg", kv_store_ev[ssl])
            if qe == "a":
                evv = OP(qe, lambda En, o=qstage[hc % 2][:], i=pAB[rg][:, 0:1024]:
                         En.activation(o, i, AF.Copy, scale=QSCALE), se)
            else:
                evv = OP(qe, lambda En, o=qstage[hc % 2][:], i=pAB[rg][:, 0:1024]:
                         En.tensor_scalar_mul(o, i, QSCALE), se)
            kv_evac_ev[rg] = (se, evv)
            W("g", se, evv)
            kv_store_ev[ssl] = DMA("g", QT_s.ap()[hc, :, H * 1024:(H + 1) * 1024], qstage[hc % 2][:])

        for ch in range(2):
            sl = 0
            if wv_use_ev[sl] is not None:
                W("g", "st", wv_use_ev[sl])
            wv_ev[sl] = DMA("g", wv_sb[sl].ap().rearrange("p (dc c) -> p dc c", dc=NDC),
                            wvt.ap()[:, ch * 1024:(ch + 1) * 1024].rearrange("(dc p) c -> p dc c", p=128))
            W("t", "dg", wv_ev[sl])
            for tt in range(16):
                rg = tt % 2
                if kv_evac_ev[rg] is not None:
                    W("t", kv_evac_ev[rg][0], kv_evac_ev[rg][1])
                lastv = None
                for dc in range(NDC):
                    for ncn in range(2):
                        lastv = MM(pAB[rg][:, ncn * 512:(ncn + 1) * 512],
                                   xT[:, dc * 2048 + tt * 128: dc * 2048 + (tt + 1) * 128],
                                   wv_sb[sl][:, dc * 1024 + ncn * 512: dc * 1024 + (ncn + 1) * 512],
                                   start=(dc == 0), stop=(dc == NDC - 1),
                                   sem="st" if (dc == NDC - 1 and ncn == 1) else None)
                if tt == 15:
                    wv_use_ev[sl] = lastv
                qe = "v" if (tt % 2 == 0) else "a"
                se = "sv" if qe == "v" else "sa"
                ssl = 4 + tt % 2
                W(qe, "st", lastv)
                if kv_store_ev[ssl] is not None:
                    W(qe, "dg", kv_store_ev[ssl])
                evv = OP(qe, lambda En, o=vstage[tt % 2][:], i=pAB[rg][:, 0:1024]:
                         CP(En, o, i), se)
                kv_evac_ev[rg] = (se, evv)
                W("g", se, evv)
                dst = V_s.ap()[ch * 8:(ch + 1) * 8,
                               H * 2048 + tt * 128: H * 2048 + (tt + 1) * 128, :].rearrange("h p c -> p h c")
                src = vstage[tt % 2].ap().rearrange("p (h c) -> p h c", h=8)
                kv_store_ev[ssl] = DMA("g", dst, src)

    p2_done_dg = C["dg"]
    p2.close()

    p3 = contextlib.ExitStack()
    kbuf = [p3.enter_context(nc.sbuf_tensor(f"kbuf{i}", [128, T], BF16)) for i in range(2)]
    vbuf = [p3.enter_context(nc.sbuf_tensor(f"vbuf{i}", [128, T], BF16)) for i in range(2)]
    qbuf = p3.enter_context(nc.sbuf_tensor("qbuf", [128, 2048], BF16))
    S_sb = [p3.enter_context(nc.sbuf_tensor(f"S_sb{i}", [128, T], F32)) for i in range(2)]
    A_sb = [p3.enter_context(nc.sbuf_tensor(f"A_sb{i}", [128, T], BF16)) for i in range(2)]
    AT = p3.enter_context(nc.sbuf_tensor("AT", [128, 2 * T], BF16))
    islope = p3.enter_context(nc.sbuf_tensor("islope", [128, T], F32))
    nmax = p3.enter_context(nc.sbuf_tensor("nmax", [128, 16], F32))
    rsum = p3.enter_context(nc.sbuf_tensor("rsum", [128, 16], F32))
    rinv = p3.enter_context(nc.sbuf_tensor("rinv", [128, 16], F32))
    ps4 = p3.enter_context(nc.psum_tensor("ps4", [128, 2048], F32))
    ptb = [p3.enter_context(nc.psum_tensor(f"ptb{i}", [128, 1024], BF16)) for i in range(2)]
    po = [p3.enter_context(nc.psum_tensor(f"po{i}", [128, 512], F32)) for i in range(2)]

    W("g", "dg", p2_done_dg)
    ev_iota = DMA("g", islope[:], iota[:])
    ev_mask = DMA("g", mask_sb.ap().rearrange("p (m c) -> p m c", m=2),
                  maskd.ap().rearrange("m p c -> p m c"))

    kload_ev = [None, None]
    vload_ev = [None, None]
    qload_ev = None
    kfree_ev = [None, None]
    vfree_ev = [None, None]
    qfree_ev = None
    schunk_mm = {}
    schunk_evac = {}
    Ssb_exp_ev = [None, None]
    Asb_tr_ev = [None, None]
    ptb_evac = [None, None]
    po_evac = [None, None]
    at_evac_last = None
    av_done = None
    gchunk = 0
    stg_store_ev = [None, None]
    ot_store_done = [None] * NT

    for h in range(NH):
        hl = h % 2
        if kfree_ev[hl] is not None:
            W("g", "st", kfree_ev[hl])
        kload_ev[hl] = DMA("g", kbuf[hl][:], KT_s.ap()[h, :, :])
        if vfree_ev[hl] is not None:
            W("g", "st", vfree_ev[hl])
        vload_ev[hl] = DMA("g", vbuf[hl].ap().rearrange("p (kt c) -> p kt c", kt=32),
                           V_s.ap()[h, :, :].rearrange("(kt p) c -> p kt c", p=128))
        if qfree_ev is not None:
            W("g", "st", qfree_ev)
        qload_ev = DMA("g", qbuf[:], QT_s.ap()[h, :, :])

        if h == 0:
            W("v", "dg", ev_iota)
            OP("v", lambda En: En.tensor_scalar_mul(islope[:], islope[:], SLOPE0), "sv")
        else:
            OP("v", lambda En: En.tensor_scalar_mul(islope[:], islope[:], SLOPER), "sv")

        for s in range(8):
            nchunks = s + 1
            kpad = 512 * nchunks
            for u in range(2):
                t = 2 * s + u
                W("t", "dg", max(kload_ev[hl], qload_ev))
                if Ssb_exp_ev[u] is not None:
                    W("v", "sa", Ssb_exp_ev[u])
                ev = None
                for kc in range(nchunks):
                    slot = gchunk % 4
                    if schunk_evac.get(slot) is not None:
                        W("t", "sv", schunk_evac[slot])
                    v = MM(ps4[:, slot * 512:(slot + 1) * 512],
                           qbuf[:, t * 128:(t + 1) * 128],
                           kbuf[hl][:, kc * 512:(kc + 1) * 512],
                           start=True, stop=True, sem="st")
                    schunk_mm[slot] = v
                    if kc == nchunks - 1:
                        kfree_ev[hl] = v
                        qfree_ev = v
                    W("v", "st", v)
                    if h == 0 and s == 0 and u == 0 and kc == 0:
                        W("v", "dg", ev_mask)
                    ev = OP("v", lambda En, o=S_sb[u][:, kc * 512:(kc + 1) * 512],
                            i0=ps4[:, slot * 512:(slot + 1) * 512],
                            i1=islope[:, kc * 512:(kc + 1) * 512]:
                            En.tensor_tensor(o, i0, i1, ALU.add), "sv")
                    if kc == nchunks - 1:
                        ev = OP("v", lambda En, o=S_sb[u][:, kc * 512:(kc + 1) * 512],
                                i1=mask_sb[:, u * 512:(u + 1) * 512]:
                                En.tensor_tensor(o, o, i1, ALU.add), "sv")
                    schunk_evac[slot] = ev
                    gchunk += 1
                pm = OP("v", lambda En, o=nmax[:, t:t + 1], i=S_sb[u][:, 0:kpad]:
                        En.reduce_max(o, i, axis=AX.X, negate=True), "sv")
                W("a", "sv", pm)
                if Asb_tr_ev[u] is not None:
                    W("a", "st", Asb_tr_ev[u])
                ex = OP("a", lambda En, o=A_sb[u][:, 0:kpad], i=S_sb[u][:, 0:kpad],
                        b=nmax[:, t:t + 1], ac=rsum[:, t:t + 1]:
                        En.activation(o, i, AF.Exp, bias=b, accum_out=ac), "sa")
                Ssb_exp_ev[u] = ex
                W("v", "sa", ex)
                rc = OP("v", lambda En, o=rinv[:, t:t + 1], i=rsum[:, t:t + 1]: En.reciprocal(o, i), "sv")
                W("v", "sv", rc)
                an = OP("v", lambda En, o=A_sb[u][:, 0:kpad], s=rinv[:, t:t + 1]:
                        En.tensor_scalar_mul(o, o, s), "sv")
                W("t", "sv", an)
                if h == 0 and s == 0 and u == 0:
                    W("t", "dg", ev_idb)
                ngrp = kpad // 512
                for g in range(ngrp):
                    rg = g % 2
                    if ptb_evac[rg] is not None:
                        W("t", "sa", ptb_evac[rg])
                    lastv = None
                    for c4 in range(4):
                        kt = g * 4 + c4
                        lastv = TR(ptb[rg][:, c4 * 128:(c4 + 1) * 128],
                                   A_sb[u][:, kt * 128:(kt + 1) * 128], idb_sb[:],
                                   sem="st" if c4 == 3 else None)
                    if g == ngrp - 1:
                        Asb_tr_ev[u] = lastv
                    W("a", "st", lastv)
                    if av_done is not None:
                        W("a", "st", av_done)
                    pe = OP("a", lambda En, o=AT[:, u * T + g * 512: u * T + (g + 1) * 512],
                            i=ptb[rg][:, 0:512]: En.copy(o, i), "sa")
                    ptb_evac[rg] = pe
                    at_evac_last = pe
            rg = s % 2
            if po_evac[rg] is not None:
                W("t", "sa", po_evac[rg])
            W("t", "sa", at_evac_last)
            W("t", "dg", vload_ev[hl])
            ATv = AT.ap().rearrange("p (u c) -> p u c", u=2)
            pov = po[rg].ap()[:, 0:256].rearrange("p (u c) -> p u c", u=2)
            lastv = None
            for kt in range(4 * (s + 1)):
                lastv = MM(pov, vbuf[hl][:, kt * 128:(kt + 1) * 128],
                           ATv[:, :, kt * 128:(kt + 1) * 128],
                           start=(kt == 0), stop=(kt == 4 * (s + 1) - 1),
                           sem="st" if kt == 4 * (s + 1) - 1 else None)
            vfree_ev[hl] = lastv
            av_done = lastv
            par = (h * 8 + s) % 2
            W("a", "st", lastv)
            if stg_store_ev[par] is not None:
                W("a", "dg", stg_store_ev[par])
            pe = OP("a", lambda En, o=ot_stg[par][:], i=po[rg][:, 0:256]: En.copy(o, i), "sa")
            po_evac[rg] = pe
            W("g", "sa", pe)
            DMA("g", OT_s.ap()[2 * s, h, :, :], ot_stg[par].ap()[:, 0:128])
            ev2 = DMA("g", OT_s.ap()[2 * s + 1, h, :, :], ot_stg[par].ap()[:, 128:256])
            stg_store_ev[par] = ev2
            ot_store_done[2 * s] = ev2
            ot_store_done[2 * s + 1] = ev2

    p3_done_a = C["sa"]
    p3.close()

    p41 = contextlib.ExitStack()
    wot_sb = p41.enter_context(nc.sbuf_tensor("wot_sb", [128, NH * 2048], BF16))
    xrows = [p41.enter_context(nc.sbuf_tensor(f"xrows{i}", [128, D], F32)) for i in range(2)]
    zstage = [p41.enter_context(nc.sbuf_tensor(f"zstage{i}", [128, D], F32)) for i in range(2)]
    sqstage = p41.enter_context(nc.sbuf_tensor("sqstage", [128, D], F32))
    x2out = [p41.enter_context(nc.sbuf_tensor(f"x2out{i}", [128, D], F32)) for i in range(2)]
    pY = [p41.enter_context(nc.psum_tensor(f"pY{i}", [128, 2048], F32)) for i in range(2)]

    W("g", "sa", p3_done_a)
    wot_ev = None
    for oc in range(NH):
        wot_ev = DMA("g", wot_sb[:, oc * 2048:(oc + 1) * 2048], wot.ap()[oc * 128:(oc + 1) * 128, :])

    pY_evac = [None, None]
    xr_ev = [None, None]
    x2st_ev = [None, None]
    zs_used = [None, None]
    otld_ev = [None] * 4
    otmm_done = [None] * 4
    for tl in range(NT):
        rg = tl % 4
        if otmm_done[rg] is not None:
            W("g", "st", otmm_done[rg])
        W("g", "dg", ot_store_done[tl])
        otld_ev[rg] = DMA("g", OTld[rg].ap().rearrange("p (oc q) -> p oc q", oc=NH),
                          OT_s.ap()[tl].rearrange("oc p q -> p oc q"))
        pg = tl % 2
        if pY_evac[pg] is not None:
            W("a", "sv", pY_evac[pg])
        xr_ev[pg] = DMA("a", xrows[pg][:], xin[(2 * tl) * 128:(2 * tl + 1) * 128, :])
        if tl == 0:
            W("t", "sa", p3_done_a)
            W("t", "dg", wot_ev)
        if pY_evac[pg] is not None:
            W("t", "sv", pY_evac[pg])
        W("t", "dg", otld_ev[rg])
        lastv = None
        for qn in range(4):
            for oc in range(NH):
                lastv = MM(pY[pg][:, qn * 512:(qn + 1) * 512],
                           OTld[rg][:, oc * 128:(oc + 1) * 128],
                           wot_sb[:, oc * 2048 + qn * 512: oc * 2048 + (qn + 1) * 512],
                           start=(oc == 0), stop=(oc == NH - 1),
                           sem="st" if (oc == NH - 1 and qn == 3) else None)
        otmm_done[rg] = lastv
        W("v", "st", lastv)
        W("v", "dx", xr_ev[pg])
        if zs_used[pg] is not None:
            W("v", "sa", zs_used[pg])
        zev = OP("v", lambda En, o=zstage[pg][:], i0=pY[pg][:], i1=xrows[pg][:],
                 ac=small[:, 3:4]:
                 En.scalar_tensor_tensor(o, i0, 1.0, i1, ALU.mult, ALU.add, accum_out=ac), "sv")
        pY_evac[pg] = zev
        W("a", "sv", zev)
        sq = OP("a", lambda En, o=sqstage[:], i=zstage[pg][:], ac=small[:, 4:5]:
                En.activation(o, i, AF.Square, accum_out=ac), "sa")
        zs_used[pg] = sq
        W("v", "sa", sq)
        f1 = OP("v", lambda En: En.tensor_scalar_mul(small[:, 5:6], small[:, 3:4], 1.0 / D), "sv")
        W("v", "sv", f1)
        f2 = OP("v", lambda En: En.tensor_tensor(small[:, 7:8], small[:, 5:6], small[:, 5:6], ALU.mult), "sv")
        W("v", "sv", f2)
        vv = OP("v", lambda En: En.scalar_tensor_tensor(small[:, 6:7], small[:, 4:5], 1.0 / D,
                                                        small[:, 7:8], ALU.mult, ALU.subtract), "sv")
        W("a", "sv", vv)
        f3 = OP("a", lambda En: En.activation(small[:, 9:10], small[:, 6:7], AF.Ln, bias=small[:, 10:11]), "sa")
        W("a", "sa", f3)
        l2 = OP("a", lambda En: En.activation(small[:, 8:9], small[:, 9:10], AF.Exp, scale=-0.5), "sa")
        W("v", "sa", l2)
        if x2st_ev[pg] is not None:
            W("v", "dx", x2st_ev[pg])
        nv = OP("v", lambda En, o=x2out[pg][:], i=zstage[pg][:]:
                En.tensor_scalar(o, i, small[:, 5:6], small[:, 8:9], ALU.subtract, ALU.mult), "sv")
        W("a", "sv", nv)
        x2st_ev[pg] = DMA("a", x2_s[tl * 128:(tl + 1) * 128, :], x2out[pg][:])

    p41_done_dx = C["dx"]
    p41_last_mm = otmm_done[(NT - 1) % 4]
    p41.close()
    ot_stack.close()

    p42 = contextlib.ExitStack()
    histage = p42.enter_context(nc.sbuf_tensor("histage", [128, D], BF16))
    lostage = p42.enter_context(nc.sbuf_tensor("lostage", [128, D], BF16))
    x2Tst = [p42.enter_context(nc.sbuf_tensor(f"x2Tst{i}", [128, D], BF16)) for i in range(2)]
    x2TLst = [p42.enter_context(nc.sbuf_tensor(f"x2TLst{i}", [128, D], BF16)) for i in range(2)]
    ghi = p42.enter_context(nc.sbuf_tensor("ghi", [128, NDC * E], BF16))
    glo = p42.enter_context(nc.sbuf_tensor("glo", [128, NDC * E], BF16))
    sct2 = p42.enter_context(nc.sbuf_tensor("sct2", [128, 16], F32))
    gw_sb = p42.enter_context(nc.sbuf_tensor("gw_sb", [128, NDC * E], F32))
    x2stage = [p42.enter_context(nc.sbuf_tensor(f"x2stage{i}", [128, D], F32)) for i in range(2)]
    p42.enter_context(nc.sbuf_tensor("p42pad", [128, 12096], mybir.dt.uint8))
    ptb2 = [p42.enter_context(nc.psum_tensor(f"ptb2{i}", [128, 1024], BF16)) for i in range(2)]
    psc = p42.enter_context(nc.psum_tensor("psc", [128, 512], F32))

    W("g", "st", p41_last_mm)
    W("g", "dx", p41_done_dx)
    gw_ev = DMA("g", gw_sb.ap().rearrange("p (dc e) -> p dc e", dc=NDC),
                gwt.ap().rearrange("(dc p) e -> p dc e", p=128))
    W("a", "dg", gw_ev)
    g1 = OP("a", lambda En: En.copy(ghi[:], gw_sb[:]), "sa")
    W("v", "sa", g1)
    OP("v", lambda En: En.tensor_tensor(glo[:], gw_sb[:], ghi[:], ALU.subtract), "sv")

    x2l_ev = [None, None]
    x2free = [None, None]
    hi_tr_last = None
    lo_tr_last = None
    tr2_evac = [None, None]
    x2T_store = [None, None]
    sg_prev = None
    top2_prev = None
    stage_free = [None, None]
    for tl in range(NT):
        rg = tl % 2
        if stage_free[rg] is not None:
            W("g", "st", stage_free[rg])
        if x2free[rg] is not None:
            W("g", "sv", x2free[rg])
        if tl < 2:
            W("g", "dx", p41_done_dx)
        x2l_ev[rg] = DMA("g", x2stage[rg][:], x2_s[tl * 128:(tl + 1) * 128, :])
        W("a", "dg", x2l_ev[rg])
        if hi_tr_last is not None:
            W("a", "st", hi_tr_last)
        hi_ev = OP("a", lambda En, o=histage[:], i=x2stage[rg][:]: En.copy(o, i), "sa")
        W("v", "sa", hi_ev)
        if lo_tr_last is not None:
            W("v", "st", lo_tr_last)
        lo_ev = OP("v", lambda En, o=lostage[:], i0=x2stage[rg][:], i1=histage[:]:
                   En.tensor_tensor(o, i0, i1, ALU.subtract), "sv")
        x2free[rg] = lo_ev
        hi_done = lo_done = None
        for (src, dst, evdep, which) in ((histage, x2Tst[rg], ("sa", hi_ev), 0),
                                         (lostage, x2TLst[rg], ("sv", lo_ev), 1)):
            W("t", evdep[0], evdep[1])
            for g in range(4):
                prg = g % 2
                if tr2_evac[prg] is not None:
                    W("t", tr2_evac[prg][0], tr2_evac[prg][1])
                lastv = None
                for c4 in range(4):
                    c = g * 4 + c4
                    lastv = TR(ptb2[prg][:, c4 * 128:(c4 + 1) * 128],
                               src[:, c * 128:(c + 1) * 128], idb_sb[:],
                               sem="st" if c4 == 3 else None)
                if g == 3:
                    if which == 0:
                        hi_tr_last = lastv
                    else:
                        lo_tr_last = lastv
                qe = "v" if (g % 2 == 0) else "a"
                se = "sv" if qe == "v" else "sa"
                W(qe, "st", lastv)
                if which == 0 and x2T_store[rg] is not None:
                    W(qe, "dg", x2T_store[rg])
                evv = OP(qe, lambda En, o=dst[:, g * 512:(g + 1) * 512], i=ptb2[prg][:, 0:512]:
                         CP(En, o, i), se)
                tr2_evac[prg] = (se, evv)
                if which == 0 and g == 3:
                    hi_done = (se, evv)
                if which == 1 and g == 3:
                    lo_done = (se, evv)
        W("g", hi_done[0], hi_done[1])
        x2T_store[rg] = DMA("g", x2T_s.ap()[:, tl * 128:(tl + 1) * 128].rearrange("(dc p) c -> p dc c", p=128),
                            x2Tst[rg].ap().rearrange("p (dc c) -> p dc c", dc=NDC))
        W("t", hi_done[0], hi_done[1])
        W("t", lo_done[0], lo_done[1])
        if sg_prev is not None:
            W("t", "sa", sg_prev)
        nterm = 0
        lastv = None
        for (xsb, gsb) in ((x2Tst[rg], ghi), (x2Tst[rg], glo), (x2TLst[rg], ghi)):
            for dc in range(NDC):
                lastv = MM(psc[:, 0:16], xsb[:, dc * 128:(dc + 1) * 128],
                           gsb[:, dc * E:(dc + 1) * E],
                           start=(nterm == 0), stop=(nterm == 3 * NDC - 1),
                           sem="st" if nterm == 3 * NDC - 1 else None)
                nterm += 1
        stage_free[rg] = lastv
        W("a", "st", lastv)
        if top2_prev is not None:
            W("a", "sv", top2_prev)
        sg = OP("a", lambda En: En.activation(sct2[:], psc[:, 0:16], AF.Sigmoid), "sa")
        sg_prev = sg
        W("v", "sa", sg)
        m1 = small[:, 16:17]; m2 = small[:, 17:18]; s12 = small[:, 18:19]; ri = small[:, 19:20]
        w1n = small[:, 20:21]; w2n = small[:, 21:22]
        eq1 = small[:, 32:48]; msk = small[:, 48:64]

        def FOP(fn):
            ev = OP("v", fn, "sv")
            W("v", "sv", ev)
            return ev
        FOP(lambda En: En.reduce_max(m1, sct2[:], axis=AX.X))
        FOP(lambda En: En.tensor_scalar(eq1, sct2[:], m1, None, ALU.is_equal))
        FOP(lambda En: En.scalar_tensor_tensor(msk, eq1, NEG, sct2[:], ALU.mult, ALU.add))
        FOP(lambda En: En.reduce_max(m2, msk, axis=AX.X))
        FOP(lambda En: En.tensor_tensor(s12, m1, m2, ALU.add))
        FOP(lambda En: En.reciprocal(ri, s12))
        FOP(lambda En: En.tensor_tensor(w1n, m1, ri, ALU.mult))
        FOP(lambda En: En.tensor_tensor(w2n, m2, ri, ALU.mult))
        FOP(lambda En: En.tensor_scalar(msk, msk, m2, None, ALU.is_equal))
        FOP(lambda En, o=dw_sb[:, tl * E:(tl + 1) * E]: En.tensor_scalar(o, eq1, w1n, None, ALU.mult))
        top2_prev = OP("v", lambda En, o=dw_sb[:, tl * E:(tl + 1) * E]:
                       En.scalar_tensor_tensor(o, msk, w2n, o, ALU.mult, ALU.add), "sv")

    p42_done_dg = C["dg"]
    p42.close()

    p5 = contextlib.ExitStack()
    silu_sb = [p5.enter_context(nc.sbuf_tensor(f"silu{i}", [128, GT], F32)) for i in range(2)]
    x2r = [p5.enter_context(nc.sbuf_tensor(f"x2r{i}", [128, D], F32)) for i in range(2)]
    yacc = p5.enter_context(nc.sbuf_tensor("yacc", [128, 8 * D], F32))
    x2Tg = p5.enter_context(nc.sbuf_tensor("x2Tg", [128, NDC * GT], BF16))
    hT = p5.enter_context(nc.sbuf_tensor("hT", [128, 6 * GT], BF16))
    w13b = [p5.enter_context(nc.sbuf_tensor(f"w13b{i}", [128, 2 * NDC * 128], BF16)) for i in range(2)]
    w2b = p5.enter_context(nc.sbuf_tensor("w2b", [128, 6 * D], BF16))
    outst = [p5.enter_context(nc.sbuf_tensor(f"outst{i}", [128, D], F32)) for i in range(2)]
    pR = [p5.enter_context(nc.psum_tensor(f"pR{i}", [128, 2048], F32)) for i in range(2)]

    units = [("s", None, 0, "init"), ("s", None, 11, "add")]
    units += [("e", e, 0, "dw") for e in range(E)]

    pR_read = [None, None]
    w13_use = [None, None]
    w13_ld = [None, None]
    w2_use = None
    w2_ld = [None] * 6
    hT_use = [None] * 6
    silu_use = [None, None]
    out_store = [None, None]
    x2r_used = [None, None]
    last_ab_grp = None
    strip_ctr = 0

    def w13_src(kind, e, i):
        if kind == "s":
            return sw13.ap()[i, :, :, :]
        return w13.ap()[e, i, :, :, :]

    def w2_src(kind, e, i):
        if kind == "s":
            return sw2r.ap()[i * 128:(i + 1) * 128, :]
        return w2r.ap()[e, i * 128:(i + 1) * 128, :]

    W("g", "dx", p41_done_dx)
    W("t", "sv", top2_prev)
    W("t", "sa", sg_prev)
    W("s", "sv", C["sv"])
    W("s", "sa", C["sa"])
    W("s", "st", C["st"])
    W("s", "dg", C["dg"])
    for grp in range(2):
        W("g", "dg", p42_done_dg)
        if last_ab_grp is not None:
            W("g", "st", last_ab_grp)
        gx_ev = DMA("g", x2Tg.ap().rearrange("p (dc c) -> p dc c", dc=NDC),
                    x2T_s.ap()[:, grp * GT:(grp + 1) * GT].rearrange("(dc p) c -> p dc c", p=128))
        first_unit = True
        for (ukind, ue, ubase, uacc) in units:
            for half, (hbase, hn) in enumerate(((0, 6), (6, 5))):
                for ii in range(hn):
                    i = ubase + hbase + ii
                    if w2_use is not None:
                        W("s", "st", w2_use)
                        w2_use = None
                    w2_ld[ii] = DMA("s", w2b[:, ii * D:(ii + 1) * D], w2_src(ukind, ue, i))
                for ii in range(hn):
                    i = ubase + hbase + ii
                    sl = strip_ctr % 2
                    if w13_use[sl] is not None:
                        W("s", "st", w13_use[sl])
                    w13_ld[sl] = DMA("s", w13b[sl].ap().rearrange("p (m dc c) -> p m dc c", m=2, dc=NDC),
                                     w13_src(ukind, ue, i).rearrange("m (dc p) c -> p m dc c", p=128))
                    rg = strip_ctr % 2
                    if pR_read[rg] is not None:
                        W("t", pR_read[rg][0], pR_read[rg][1])
                    W("t", "ds", w13_ld[sl])
                    if first_unit and ii == 0 and half == 0:
                        W("t", "dg", gx_ev)
                    lastv = None
                    for m in range(2):
                        for dc in range(NDC):
                            for qn in range(2):
                                lastv = MM(pR[rg][:, m * 1024 + qn * 512: m * 1024 + (qn + 1) * 512],
                                           w13b[sl][:, (m * NDC + dc) * 128:(m * NDC + dc + 1) * 128],
                                           x2Tg[:, dc * GT + qn * 512: dc * GT + (qn + 1) * 512],
                                           start=(dc == 0), stop=(dc == NDC - 1),
                                           sem="st" if (m == 1 and dc == NDC - 1 and qn == 1) else None)
                    w13_use[sl] = lastv
                    last_ab_grp = lastv
                    ssl = strip_ctr % 2
                    W("a", "st", lastv)
                    if silu_use[ssl] is not None:
                        W("a", "sv", silu_use[ssl])
                    sv_ = OP("a", lambda En, o=silu_sb[ssl][:], i=pR[rg][:, 0:1024]:
                             En.activation(o, i, AF.Silu), "sa")
                    W("v", "sa", sv_)
                    if hT_use[ii] is not None:
                        W("v", "st", hT_use[ii])
                    hm = OP("v", lambda En, o=hT[:, ii * GT:(ii + 1) * GT], i0=silu_sb[ssl][:],
                            i1=pR[rg][:, 1024:2048]: En.tensor_tensor(o, i0, i1, ALU.mult), "sv")
                    silu_use[ssl] = hm
                    pR_read[rg] = ("sv", hm)
                    strip_ctr += 1
                hm_last = hm
                first_unit = False
                for tt in range(8):
                    rg = tt % 2
                    if pR_read[rg] is not None:
                        W("t", pR_read[rg][0], pR_read[rg][1])
                    W("t", "sv", hm_last)
                    W("t", "ds", w2_ld[hn - 1])
                    lastv = None
                    for ii in range(hn):
                        for qn in range(4):
                            lastv = MM(pR[rg][:, qn * 512:(qn + 1) * 512],
                                       hT[:, ii * GT + tt * 128: ii * GT + (tt + 1) * 128],
                                       w2b[:, ii * D + qn * 512: ii * D + (qn + 1) * 512],
                                       start=(ii == 0), stop=(ii == hn - 1),
                                       sem="st" if (ii == hn - 1 and qn == 3) else None)
                        if tt == 7:
                            hT_use[ii] = lastv
                    if tt == 7:
                        w2_use = lastv
                    W("v", "st", lastv)
                    tl = grp * 8 + tt
                    ysl = yacc[:, tt * D:(tt + 1) * D]
                    if uacc == "init" and half == 0:
                        yev = OP("v", lambda En, o=ysl, i=pR[rg][:]: En.tensor_copy(o, i), "sv")
                    elif uacc in ("init", "add"):
                        yev = OP("v", lambda En, o=ysl, i=pR[rg][:]:
                                 En.scalar_tensor_tensor(o, i, 1.0, o, ALU.mult, ALU.add), "sv")
                    else:
                        dcol = dw_sb[:, tl * E + ue: tl * E + ue + 1]
                        yev = OP("v", lambda En, o=ysl, i=pR[rg][:], s=dcol:
                                 En.scalar_tensor_tensor(o, i, s, o, ALU.mult, ALU.add), "sv")
                    pR_read[rg] = ("sv", yev)
        for tt in range(8):
            rg = tt % 2
            tl = grp * 8 + tt
            if x2r_used[rg] is not None:
                W("g", x2r_used[rg][0], x2r_used[rg][1])
            xe = DMA("g", x2r[rg][:], x2_s[tl * 128:(tl + 1) * 128, :])
            W("v", "dg", xe)
            ysl = yacc[:, tt * D:(tt + 1) * D]
            z2 = OP("v", lambda En, o=ysl, i=x2r[rg][:]:
                    En.scalar_tensor_tensor(o, o, 1.0, i, ALU.mult, ALU.add,
                                            accum_out=small[:, 3:4]), "sv")
            W("a", "sv", z2)
            sq = OP("a", lambda En, o=x2r[rg][:], i=ysl, ac=small[:, 4:5]:
                    En.activation(o, i, AF.Square, accum_out=ac), "sa")
            x2r_used[rg] = ("sa", sq)
            W("v", "sa", sq)
            f1 = OP("v", lambda En: En.tensor_scalar_mul(small[:, 5:6], small[:, 3:4], 1.0 / D), "sv")
            W("v", "sv", f1)
            f2 = OP("v", lambda En: En.tensor_tensor(small[:, 7:8], small[:, 5:6], small[:, 5:6], ALU.mult), "sv")
            W("v", "sv", f2)
            vv = OP("v", lambda En: En.scalar_tensor_tensor(small[:, 6:7], small[:, 4:5], 1.0 / D,
                                                            small[:, 7:8], ALU.mult, ALU.subtract), "sv")
            W("a", "sv", vv)
            f3 = OP("a", lambda En: En.activation(small[:, 9:10], small[:, 6:7], AF.Ln, bias=small[:, 10:11]), "sa")
            W("a", "sa", f3)
            l2 = OP("a", lambda En: En.activation(small[:, 8:9], small[:, 9:10], AF.Exp, scale=-0.5), "sa")
            W("v", "sa", l2)
            if out_store[rg] is not None:
                W("v", "do", out_store[rg])
            nv = OP("v", lambda En, o=outst[rg][:], i=ysl:
                    En.tensor_scalar(o, i, small[:, 5:6], small[:, 8:9], ALU.subtract, ALU.mult), "sv")
            W("s", "sv", nv)
            out_store[rg] = DMAo(out[tl * 128:(tl + 1) * 128, :], outst[rg][:])

    Q["s"].append(("w", "do", C["do"]))
    p5.close()

    with (
        nc.semaphore("st") as h_st, nc.semaphore("sv") as h_sv,
        nc.semaphore("sa") as h_sa, nc.semaphore("sp") as h_sp,
        nc.semaphore("dg") as h_dg, nc.semaphore("ds") as h_ds,
        nc.semaphore("dx") as h_dx, nc.semaphore("do") as h_do,
        nc.Block() as block,
    ):
        SEM = {"st": h_st, "sv": h_sv, "sa": h_sa, "sp": h_sp,
               "dg": h_dg, "ds": h_ds, "dx": h_dx, "do": h_do}

        def runq(En, key):
            for item in Q[key]:
                if item[0] == "w":
                    En.wait_ge(SEM[item[1]], item[2])
                else:
                    ins = item[1](En)
                    if item[2] is not None:
                        ins.then_inc(SEM[item[2]], item[3])

        block.tensor(lambda En: runq(En, "t"))
        block.vector(lambda En: runq(En, "v"))
        block.scalar(lambda En: runq(En, "a"))
        block.gpsimd(lambda En: runq(En, "g"))
        block.sync(lambda En: runq(En, "s"))

    outer.close()
    return nc


def _prep(inputs):
    bf = ml_dtypes.bfloat16
    Wq = np.asarray(inputs["L0_attn_Wq"], np.float32)
    Wk = np.asarray(inputs["L0_attn_Wk"], np.float32)
    Wv = np.asarray(inputs["L0_attn_Wv"], np.float32)
    Wo = np.asarray(inputs["L0_attn_Wo"], np.float32)
    gw = np.asarray(inputs["L0_ffn_gate_w"], np.float32)
    w1 = np.asarray(inputs["L0_ffn_w1"])
    w2 = np.asarray(inputs["L0_ffn_w2"])
    w3 = np.asarray(inputs["L0_ffn_w3"])
    sw1 = np.asarray(inputs["L0_ffn_sw1"])
    sw2 = np.asarray(inputs["L0_ffn_sw2"])
    sw3 = np.asarray(inputs["L0_ffn_sw3"])

    def slabify(w):
        wt = np.ascontiguousarray(np.swapaxes(w, -1, -2))
        nI = wt.shape[-1] // 128
        s = wt.reshape(wt.shape[:-2] + (D, nI, 128))
        s = np.moveaxis(s, -2, -3)
        return np.ascontiguousarray(s)

    w1s = slabify(w1)
    w3s = slabify(w3)
    w13 = np.ascontiguousarray(np.stack([w1s, w3s], axis=2))
    sw1s = slabify(sw1)
    sw3s = slabify(sw3)
    sw13 = np.ascontiguousarray(np.stack([sw1s, sw3s], axis=1))
    w2r = np.ascontiguousarray(np.swapaxes(w2, -1, -2))
    sw2r = np.ascontiguousarray(sw2.T)

    return {
        "wqt": np.ascontiguousarray(Wq.T), "wkt": np.ascontiguousarray(Wk.T),
        "wvt": np.ascontiguousarray(Wv.T), "wot": np.ascontiguousarray(Wo.T),
        "gwt": np.ascontiguousarray(gw.T),
        "w13": w13, "w2r": w2r, "sw13": sw13, "sw2r": sw2r,
        "idf": np.eye(128, dtype=np.float32),
        "idb": np.eye(128, dtype=bf),
    }


def _core_maps(inputs, base):
    x = np.asarray(inputs["L0_attn_x"], np.float32)
    maps = []
    for c in range(8):
        b, j = c // 2, c % 2
        perm = np.empty(32, np.int64)
        for t in range(16):
            perm[2 * t] = 2 * t + j
            perm[2 * t + 1] = 2 * t + (1 - j)
        xb = np.ascontiguousarray(x[b].reshape(32, 128, D)[perm].reshape(T, D))
        kidx = (perm[:, None] * 128 + np.arange(128)[None, :]).reshape(T).astype(np.float32)
        iota = np.ascontiguousarray(np.broadcast_to(kidx, (128, T)))
        maskd = np.zeros((2, 128, 512), np.float32)
        qq = np.arange(128)[:, None]
        for par in range(2):
            t = par
            g_q = 2 * t + j
            cs = 4 * (t // 2)
            gk = (perm[cs:cs + 4, None] * 128 + np.arange(128)[None, :]).reshape(512)
            maskd[par] = np.where(gk[None, :] > (g_q * 128 + qq), np.float32(NEG), np.float32(0.0))
        m = dict(base)
        m["xb"] = xb
        m["iota"] = iota
        m["maskd"] = maskd
        maps.append(m)
    return maps


def kernel(**inputs):
    base = _prep(inputs)
    maps = _core_maps(inputs, base)
    nc = build()
    res = run_bass_kernel_spmd(nc, maps, core_ids=list(range(8)))
    outf = np.empty((B, T, D), np.float32)
    for c in range(8):
        b, j = c // 2, c % 2
        o = np.asarray(res.results[c]["out"])
        for t in range(16):
            g = 2 * t + j
            outf[b, g * 128:(g + 1) * 128, :] = o[t * 128:(t + 1) * 128, :]
    return outf


# version 03
# speedup vs baseline: 1.0596x; 1.0596x over previous
import math
import numpy as np
import ml_dtypes
import concourse.bass as bass
import concourse.mybir as mybir
from concourse.bass_utils import run_bass_kernel_spmd

F32 = mybir.dt.float32
BF16 = mybir.dt.bfloat16
AF = mybir.ActivationFunctionType
ALU = mybir.AluOpType
AX = mybir.AxisListType

B, T, D = 4, 4096, 2048
NH, HD = 16, 128
NDC = D // 128
E, EI, NSTR = 16, 1408, 11
SI, SSTR = 2816, 22
NT = 16
GT = 1024
NEG = -1e30
EPS = 1e-5
QSCALE = 1.0 / math.sqrt(128.0)
SLOPE0 = 0.5 ** (8.0 / 16.0)
SLOPER = 0.5 ** (8.0 / 16.0)


def build(debug=False):
    nc = bass.Bass("TRN2", target_bir_lowering=False)

    xin = nc.dram_tensor("xb", [T, D], F32, kind="ExternalInput")
    wqt = nc.dram_tensor("wqt", [D, D], F32, kind="ExternalInput")
    wkt = nc.dram_tensor("wkt", [D, D], F32, kind="ExternalInput")
    wvt = nc.dram_tensor("wvt", [D, D], F32, kind="ExternalInput")
    wot = nc.dram_tensor("wot", [D, D], F32, kind="ExternalInput")
    gwt = nc.dram_tensor("gwt", [D, E], F32, kind="ExternalInput")
    w13 = nc.dram_tensor("w13", [E, NSTR, 2, D, 128], BF16, kind="ExternalInput")
    w2r = nc.dram_tensor("w2r", [E, EI, D], BF16, kind="ExternalInput")
    sw13 = nc.dram_tensor("sw13", [SSTR, 2, D, 128], BF16, kind="ExternalInput")
    sw2r = nc.dram_tensor("sw2r", [SI, D], BF16, kind="ExternalInput")
    iota = nc.dram_tensor("iota", [128, T], F32, kind="ExternalInput")
    maskd = nc.dram_tensor("maskd", [2, 128, 512], F32, kind="ExternalInput")
    idf = nc.dram_tensor("idf", [128, 128], F32, kind="ExternalInput")
    idb = nc.dram_tensor("idb", [128, 128], BF16, kind="ExternalInput")
    out = nc.dram_tensor("out", [NT * 128, D], F32, kind="ExternalOutput")

    skind = "ExternalOutput" if debug else "Internal"
    KT_s = nc.dram_tensor("KT_s", [NH, 128, T], BF16, kind=skind)
    V_s = nc.dram_tensor("V_s", [NH, T, 128], BF16, kind=skind)
    QT_s = nc.dram_tensor("QT_s", [NH, 128, NT * 128], BF16, kind=skind)
    x2_s = nc.dram_tensor("x2_s", [NT * 128, D], F32, kind=skind)
    x2T_s = nc.dram_tensor("x2T_s", [D, NT * 128], BF16, kind=skind)
    OT_s = nc.dram_tensor("OT_s", [NT, NH, 128, 128], BF16, kind=skind)

    Q = {"t": [], "v": [], "a": [], "g": [], "s": []}
    C = {k: 0 for k in ("st", "sv", "sa", "sp", "dg", "ds", "dx", "do")}

    def W(q, sem, val):
        if val:
            Q[q].append(("w", sem, val))

    def OP(q, fn, sem=None, n=1):
        if sem is None:
            Q[q].append(("i", fn, None, 0))
            return None
        C[sem] += n
        Q[q].append(("i", fn, sem, n))
        return C[sem]

    def CP(En, o, i):
        if hasattr(En, "tensor_copy"):
            return En.tensor_copy(o, i)
        return En.copy(o, i)

    def MM(out_ap, lhsT, rhs, start, stop, sem=None):
        return OP("t", lambda En, o=out_ap, l=lhsT, r=rhs, s1=start, s2=stop:
                  En.matmul(o, l, r, start=s1, stop=s2), sem)

    def TR(out_ap, in_ap, ident, sem=None):
        return OP("t", lambda En, o=out_ap, i=in_ap, d=ident: En.transpose(o, i, d), sem)

    def DMA(q, out_ap, in_ap):
        sem = {"g": "dg", "s": "ds", "a": "dx"}[q]
        C[sem] += 16
        Q[q].append(("i", lambda En, o=out_ap, i=in_ap: En.dma_start(o, i), sem, 16))
        return C[sem]

    def DMAo(out_ap, in_ap):
        C["do"] += 16
        Q["s"].append(("i", lambda En, o=out_ap, i=in_ap: En.dma_start(o, i), "do", 16))
        return C["do"]

    import contextlib
    outer = contextlib.ExitStack()
    idf_sb = outer.enter_context(nc.sbuf_tensor("idf_sb", [128, 128], F32))
    idb_sb = outer.enter_context(nc.sbuf_tensor("idb_sb", [128, 128], BF16))
    dw_sb = outer.enter_context(nc.sbuf_tensor("dw_sb", [128, NT * E], F32))
    small = outer.enter_context(nc.sbuf_tensor("small", [128, 64], F32))
    ot_stack = contextlib.ExitStack()
    mask_sb = ot_stack.enter_context(nc.sbuf_tensor("mask_sb", [128, 1024], F32))
    ot_stg = [ot_stack.enter_context(nc.sbuf_tensor(f"ot_stg{i}", [128, 256], BF16)) for i in range(2)]
    kstage = [ot_stack.enter_context(nc.sbuf_tensor(f"kstage{i}", [128, 2048], BF16)) for i in range(2)]
    vstage = [ot_stack.enter_context(nc.sbuf_tensor(f"vstage{i}", [128, 1024], BF16)) for i in range(2)]
    qstage = [ot_stack.enter_context(nc.sbuf_tensor(f"qstage{i}", [128, 1024], BF16)) for i in range(2)]
    OTld = [ot_stack.enter_context(nc.sbuf_tensor(f"OTld{i}", [128, NH * 128], BF16)) for i in range(4)]
    ot_stack.enter_context(nc.sbuf_tensor("otpad", [128, 17280], mybir.dt.uint8))

    ev_idf = DMA("g", idf_sb[:], idf[:])
    ev_idb = DMA("g", idb_sb[:], idb[:])
    OP("v", lambda En: En.memset(small[:, 10:11], EPS), "sv")

    p2 = contextlib.ExitStack()
    xT = p2.enter_context(nc.sbuf_tensor("xT", [128, NDC * 2048], BF16))
    xstage = [p2.enter_context(nc.sbuf_tensor(f"xstage{i}", [128, D], F32)) for i in range(2)]
    wslab = [p2.enter_context(nc.sbuf_tensor(f"wslab{i}", [128, NDC * 128], BF16)) for i in range(2)]
    wv_sb = [p2.enter_context(nc.sbuf_tensor("wv_sb0", [128, NDC * 1024], BF16))] * 2
    pA = p2.enter_context(nc.psum_tensor("pA", [128, 2048], F32))
    pB = p2.enter_context(nc.psum_tensor("pB", [128, 2048], F32))
    pAB = [pA, pB]

    xload_ev = [None, None]
    xfree_ev = [None, None]
    trevac_ev = [None, None]
    kv_evac_ev = [None, None]
    kv_store_ev = [None] * 6
    wslab_ev = [None, None]
    wslab_use_ev = [None, None]
    wv_ev = [None, None]
    wv_use_ev = [None, None]

    for H in range(2):
        for tt in range(16):
            sl = tt % 2
            W("a", "st", xfree_ev[sl])
            xload_ev[sl] = DMA("a", xstage[sl][:],
                               xin[H * 2048 + tt * 128: H * 2048 + (tt + 1) * 128, :])
            W("t", "dx", xload_ev[sl])
            if tt == 0 and H == 0:
                W("t", "dg", ev_idf)
            for g in range(4):
                rg = g % 2
                if trevac_ev[rg] is not None:
                    W("t", trevac_ev[rg][0], trevac_ev[rg][1])
                lastv = None
                for c4 in range(4):
                    c = g * 4 + c4
                    lastv = TR(pAB[rg][:, c4 * 128:(c4 + 1) * 128],
                               xstage[sl][:, c * 128:(c + 1) * 128], idf_sb[:],
                               sem="st" if c4 == 3 else None)
                if g == 3:
                    xfree_ev[sl] = lastv
                qe = "v" if (g % 2 == 0) else "a"
                se = "sv" if qe == "v" else "sa"
                W(qe, "st", lastv)
                dst = xT.ap().rearrange("p (dc c) -> p dc c", dc=NDC)[:, g * 4:(g + 1) * 4,
                                                                     tt * 128:(tt + 1) * 128]
                src = pAB[rg].ap()[:, 0:512].rearrange("p (q c) -> p q c", q=4)
                evv = OP(qe, lambda En, o=dst, i=src: CP(En, o, i), se)
                trevac_ev[rg] = (se, evv)
        for rg in range(2):
            if trevac_ev[rg] is not None:
                W("t", trevac_ev[rg][0], trevac_ev[rg][1])
                trevac_ev[rg] = None

        for hc in range(NH):
            sl = hc % 2
            if wslab_use_ev[sl] is not None:
                W("g", "st", wslab_use_ev[sl])
            wslab_ev[sl] = DMA("g", wslab[sl].ap().rearrange("p (dc c) -> p dc c", dc=NDC),
                               wkt.ap()[:, hc * 128:(hc + 1) * 128].rearrange("(dc p) c -> p dc c", p=128))
            rg = hc % 2
            if kv_evac_ev[rg] is not None:
                W("t", kv_evac_ev[rg][0], kv_evac_ev[rg][1])
            W("t", "dg", wslab_ev[sl])
            lastv = None
            for dc in range(NDC):
                for ncn in range(4):
                    lastv = MM(pAB[rg][:, ncn * 512:(ncn + 1) * 512],
                               wslab[sl][:, dc * 128:(dc + 1) * 128],
                               xT[:, dc * 2048 + ncn * 512: dc * 2048 + (ncn + 1) * 512],
                               start=(dc == 0), stop=(dc == NDC - 1),
                               sem="st" if (dc == NDC - 1 and ncn == 3) else None)
            wslab_use_ev[sl] = lastv
            qe = "v" if (hc % 2 == 0) else "a"
            se = "sv" if qe == "v" else "sa"
            ssl = hc % 2
            W(qe, "st", lastv)
            if kv_store_ev[ssl] is not None:
                W(qe, "dg", kv_store_ev[ssl])
            evv = OP(qe, lambda En, o=kstage[ssl][:], i=pAB[rg][:]: CP(En, o, i), se)
            kv_evac_ev[rg] = (se, evv)
            W("g", se, evv)
            kv_store_ev[ssl] = DMA("g", KT_s.ap()[hc, :, H * 2048:(H + 1) * 2048], kstage[ssl][:])

        xTv = xT.ap().rearrange("p (dc u par c) -> p dc u par c", dc=NDC, u=8, par=2)
        for hc in range(NH):
            sl = hc % 2
            if wslab_use_ev[sl] is not None:
                W("g", "st", wslab_use_ev[sl])
            wslab_ev[sl] = DMA("g", wslab[sl].ap().rearrange("p (dc c) -> p dc c", dc=NDC),
                               wqt.ap()[:, hc * 128:(hc + 1) * 128].rearrange("(dc p) c -> p dc c", p=128))
            rg = hc % 2
            if kv_evac_ev[rg] is not None:
                W("t", kv_evac_ev[rg][0], kv_evac_ev[rg][1])
            W("t", "dg", wslab_ev[sl])
            lastv = None
            for dc in range(NDC):
                for halfq in range(2):
                    o = pAB[rg].ap()[:, halfq * 512:(halfq + 1) * 512].rearrange("p (w c) -> p w c", w=4)
                    r = xTv[:, dc, halfq * 4:(halfq + 1) * 4, 0, :]
                    lastv = MM(o, wslab[sl][:, dc * 128:(dc + 1) * 128], r,
                               start=(dc == 0), stop=(dc == NDC - 1),
                               sem="st" if (dc == NDC - 1 and halfq == 1) else None)
            wslab_use_ev[sl] = lastv
            qe = "v" if (hc % 2 == 0) else "a"
            se = "sv" if qe == "v" else "sa"
            ssl = 2 + hc % 2
            W(qe, "st", lastv)
            if kv_store_ev[ssl] is not None:
                W(qe, "dg", kv_store_ev[ssl])
            if qe == "a":
                evv = OP(qe, lambda En, o=qstage[hc % 2][:], i=pAB[rg][:, 0:1024]:
                         En.activation(o, i, AF.Copy, scale=QSCALE), se)
            else:
                evv = OP(qe, lambda En, o=qstage[hc % 2][:], i=pAB[rg][:, 0:1024]:
                         En.tensor_scalar_mul(o, i, QSCALE), se)
            kv_evac_ev[rg] = (se, evv)
            W("g", se, evv)
            kv_store_ev[ssl] = DMA("g", QT_s.ap()[hc, :, H * 1024:(H + 1) * 1024], qstage[hc % 2][:])

        for ch in range(2):
            sl = 0
            if wv_use_ev[sl] is not None:
                W("g", "st", wv_use_ev[sl])
            wv_ev[sl] = DMA("g", wv_sb[sl].ap().rearrange("p (dc c) -> p dc c", dc=NDC),
                            wvt.ap()[:, ch * 1024:(ch + 1) * 1024].rearrange("(dc p) c -> p dc c", p=128))
            W("t", "dg", wv_ev[sl])
            for tt in range(16):
                rg = tt % 2
                if kv_evac_ev[rg] is not None:
                    W("t", kv_evac_ev[rg][0], kv_evac_ev[rg][1])
                lastv = None
                for dc in range(NDC):
                    for ncn in range(2):
                        lastv = MM(pAB[rg][:, ncn * 512:(ncn + 1) * 512],
                                   xT[:, dc * 2048 + tt * 128: dc * 2048 + (tt + 1) * 128],
                                   wv_sb[sl][:, dc * 1024 + ncn * 512: dc * 1024 + (ncn + 1) * 512],
                                   start=(dc == 0), stop=(dc == NDC - 1),
                                   sem="st" if (dc == NDC - 1 and ncn == 1) else None)
                if tt == 15:
                    wv_use_ev[sl] = lastv
                qe = "v" if (tt % 2 == 0) else "a"
                se = "sv" if qe == "v" else "sa"
                ssl = 4 + tt % 2
                W(qe, "st", lastv)
                if kv_store_ev[ssl] is not None:
                    W(qe, "dg", kv_store_ev[ssl])
                evv = OP(qe, lambda En, o=vstage[tt % 2][:], i=pAB[rg][:, 0:1024]:
                         CP(En, o, i), se)
                kv_evac_ev[rg] = (se, evv)
                W("g", se, evv)
                dst = V_s.ap()[ch * 8:(ch + 1) * 8,
                               H * 2048 + tt * 128: H * 2048 + (tt + 1) * 128, :].rearrange("h p c -> p h c")
                src = vstage[tt % 2].ap().rearrange("p (h c) -> p h c", h=8)
                kv_store_ev[ssl] = DMA("g", dst, src)

    p2_done_dg = C["dg"]
    p2.close()

    p3 = contextlib.ExitStack()
    kbuf = [p3.enter_context(nc.sbuf_tensor(f"kbuf{i}", [128, T], BF16)) for i in range(2)]
    vbuf = [p3.enter_context(nc.sbuf_tensor(f"vbuf{i}", [128, T], BF16)) for i in range(2)]
    qbuf = p3.enter_context(nc.sbuf_tensor("qbuf", [128, 2048], BF16))
    S_sb = [p3.enter_context(nc.sbuf_tensor(f"S_sb{i}", [128, T], F32)) for i in range(2)]
    A_sb = [p3.enter_context(nc.sbuf_tensor(f"A_sb{i}", [128, T], BF16)) for i in range(2)]
    AT = p3.enter_context(nc.sbuf_tensor("AT", [128, 2 * T], BF16))
    islope = p3.enter_context(nc.sbuf_tensor("islope", [128, T], F32))
    nmax = p3.enter_context(nc.sbuf_tensor("nmax", [128, 16], F32))
    cmax = p3.enter_context(nc.sbuf_tensor("cmax", [128, 16], F32))
    rsum = p3.enter_context(nc.sbuf_tensor("rsum", [128, 16], F32))
    rinv = p3.enter_context(nc.sbuf_tensor("rinv", [128, 16], F32))
    ps4 = p3.enter_context(nc.psum_tensor("ps4", [128, 2048], F32))
    ptb = [p3.enter_context(nc.psum_tensor(f"ptb{i}", [128, 1024], BF16)) for i in range(2)]
    po = [p3.enter_context(nc.psum_tensor(f"po{i}", [128, 512], F32)) for i in range(2)]

    W("g", "dg", p2_done_dg)
    ev_iota = DMA("g", islope[:], iota[:])
    ev_mask = DMA("g", mask_sb.ap().rearrange("p (m c) -> p m c", m=2),
                  maskd.ap().rearrange("m p c -> p m c"))

    kload_ev = [None, None]
    vload_ev = [None, None]
    qload_ev = None
    kfree_ev = [None, None]
    vfree_ev = [None, None]
    qfree_ev = None
    schunk_mm = {}
    schunk_evac = {}
    Ssb_exp_ev = [None, None]
    Asb_tr_ev = [None, None]
    ptb_evac = [None, None]
    po_evac = [None, None]
    at_evac_last = None
    av_done = None
    gchunk = 0
    stg_store_ev = [None, None]
    ot_store_done = [None] * NT

    for h in range(NH):
        hl = h % 2
        if kfree_ev[hl] is not None:
            W("g", "st", kfree_ev[hl])
        kload_ev[hl] = DMA("g", kbuf[hl][:], KT_s.ap()[h, :, :])
        if vfree_ev[hl] is not None:
            W("g", "st", vfree_ev[hl])
        vload_ev[hl] = DMA("g", vbuf[hl].ap().rearrange("p (kt c) -> p kt c", kt=32),
                           V_s.ap()[h, :, :].rearrange("(kt p) c -> p kt c", p=128))
        if qfree_ev is not None:
            W("g", "st", qfree_ev)
        qload_ev = DMA("g", qbuf[:], QT_s.ap()[h, :, :])

        if h == 0:
            W("v", "dg", ev_iota)
            OP("v", lambda En: En.tensor_scalar_mul(islope[:], islope[:], SLOPE0), "sv")
        else:
            OP("v", lambda En: En.tensor_scalar_mul(islope[:], islope[:], SLOPER), "sv")

        pend = {}

        def issue_scores(s, u):
            nonlocal gchunk, qfree_ev
            t = 2 * s + u
            nchunks = s + 1
            W("t", "dg", max(kload_ev[hl], qload_ev))
            recs = []
            for kc in range(nchunks):
                slot = gchunk % 4
                if schunk_evac.get(slot) is not None:
                    W("t", "sv", schunk_evac[slot])
                v = MM(ps4[:, slot * 512:(slot + 1) * 512],
                       qbuf[:, t * 128:(t + 1) * 128],
                       kbuf[hl][:, kc * 512:(kc + 1) * 512],
                       start=True, stop=True, sem="st")
                schunk_mm[slot] = v
                if kc == nchunks - 1:
                    kfree_ev[hl] = v
                    qfree_ev = v
                recs.append((slot, v))
                gchunk += 1
            pend[(s, u)] = recs

        def issue_softmax(s, u):
            t = 2 * s + u
            nchunks = s + 1
            kpad = 512 * nchunks
            recs = pend.pop((s, u))
            if Ssb_exp_ev[u] is not None:
                W("v", "sa", Ssb_exp_ev[u])
            for kc in range(nchunks):
                slot, v = recs[kc]
                W("v", "st", v)
                if h == 0 and s == 0 and u == 0 and kc == 0:
                    W("v", "dg", ev_mask)
                ev = OP("v", lambda En, o=S_sb[u][:, kc * 512:(kc + 1) * 512],
                        i0=ps4[:, slot * 512:(slot + 1) * 512],
                        i1=islope[:, kc * 512:(kc + 1) * 512]:
                        En.tensor_tensor(o, i0, i1, ALU.add), "sv")
                if kc == nchunks - 1:
                    ev = OP("v", lambda En, o=S_sb[u][:, kc * 512:(kc + 1) * 512],
                            i1=mask_sb[:, u * 512:(u + 1) * 512]:
                            En.tensor_tensor(o, o, i1, ALU.add), "sv")
                schunk_evac[slot] = ev
            pm = OP("v", lambda En, o=nmax[:, t:t + 1], i=S_sb[u][:, 0:kpad]:
                    En.reduce_max(o, i, axis=AX.X, negate=True), "sv")
            W("a", "sv", pm)
            if Asb_tr_ev[u] is not None:
                W("a", "st", Asb_tr_ev[u])
            ex = OP("a", lambda En, o=A_sb[u][:, 0:kpad], i=S_sb[u][:, 0:kpad],
                    b=nmax[:, t:t + 1], ac=rsum[:, t:t + 1]:
                    En.activation(o, i, AF.Exp, bias=b, accum_out=ac), "sa")
            Ssb_exp_ev[u] = ex
            W("v", "sa", ex)
            rc = OP("v", lambda En, o=rinv[:, t:t + 1], i=rsum[:, t:t + 1]: En.reciprocal(o, i), "sv")
            W("v", "sv", rc)
            an = OP("v", lambda En, o=A_sb[u][:, 0:kpad], s=rinv[:, t:t + 1]:
                    En.tensor_scalar_mul(o, o, s), "sv")
            return an

        def issue_trs(s, u, an):
            nonlocal at_evac_last
            kpad = 512 * (s + 1)
            W("t", "sv", an)
            if h == 0 and s == 0 and u == 0:
                W("t", "dg", ev_idb)
            ngrp = kpad // 512
            for g in range(ngrp):
                rg = g % 2
                if ptb_evac[rg] is not None:
                    W("t", "sa", ptb_evac[rg])
                lastv = None
                for c4 in range(4):
                    kt = g * 4 + c4
                    lastv = TR(ptb[rg][:, c4 * 128:(c4 + 1) * 128],
                               A_sb[u][:, kt * 128:(kt + 1) * 128], idb_sb[:],
                               sem="st" if c4 == 3 else None)
                if g == ngrp - 1:
                    Asb_tr_ev[u] = lastv
                W("a", "st", lastv)
                if av_done is not None:
                    W("a", "st", av_done)
                pe = OP("a", lambda En, o=AT[:, u * T + g * 512: u * T + (g + 1) * 512],
                        i=ptb[rg][:, 0:512]: En.copy(o, i), "sa")
                ptb_evac[rg] = pe
                at_evac_last = pe

        units = [(s, u) for s in range(8) for u in range(2)]
        issue_scores(*units[0])
        an_pend = {}
        for i, (s, u) in enumerate(units):
            an_pend[(s, u)] = issue_softmax(s, u)
            if i + 1 < len(units):
                issue_scores(*units[i + 1])
            issue_trs(s, u, an_pend.pop((s, u)))
            if u == 0:
                continue
            rg = s % 2
            if po_evac[rg] is not None:
                W("t", "sa", po_evac[rg])
            W("t", "sa", at_evac_last)
            W("t", "dg", vload_ev[hl])
            ATv = AT.ap().rearrange("p (u c) -> p u c", u=2)
            pov = po[rg].ap()[:, 0:256].rearrange("p (u c) -> p u c", u=2)
            lastv = None
            for kt in range(4 * (s + 1)):
                lastv = MM(pov, vbuf[hl][:, kt * 128:(kt + 1) * 128],
                           ATv[:, :, kt * 128:(kt + 1) * 128],
                           start=(kt == 0), stop=(kt == 4 * (s + 1) - 1),
                           sem="st" if kt == 4 * (s + 1) - 1 else None)
            vfree_ev[hl] = lastv
            av_done = lastv
            par = (h * 8 + s) % 2
            W("a", "st", lastv)
            if stg_store_ev[par] is not None:
                W("a", "dg", stg_store_ev[par])
            pe = OP("a", lambda En, o=ot_stg[par][:], i=po[rg][:, 0:256]: En.copy(o, i), "sa")
            po_evac[rg] = pe
            W("g", "sa", pe)
            DMA("g", OT_s.ap()[2 * s, h, :, :], ot_stg[par].ap()[:, 0:128])
            ev2 = DMA("g", OT_s.ap()[2 * s + 1, h, :, :], ot_stg[par].ap()[:, 128:256])
            stg_store_ev[par] = ev2
            ot_store_done[2 * s] = ev2
            ot_store_done[2 * s + 1] = ev2

    p3_done_a = C["sa"]
    p3.close()

    p41 = contextlib.ExitStack()
    wot_sb = p41.enter_context(nc.sbuf_tensor("wot_sb", [128, NH * 2048], BF16))
    xrows = [p41.enter_context(nc.sbuf_tensor(f"xrows{i}", [128, D], F32)) for i in range(2)]
    zstage = [p41.enter_context(nc.sbuf_tensor(f"zstage{i}", [128, D], F32)) for i in range(2)]
    sqstage = p41.enter_context(nc.sbuf_tensor("sqstage", [128, D], F32))
    x2out = [p41.enter_context(nc.sbuf_tensor(f"x2out{i}", [128, D], F32)) for i in range(2)]
    pY = [p41.enter_context(nc.psum_tensor(f"pY{i}", [128, 2048], F32)) for i in range(2)]

    W("g", "sa", p3_done_a)
    wot_ev = None
    for oc in range(NH):
        wot_ev = DMA("g", wot_sb[:, oc * 2048:(oc + 1) * 2048], wot.ap()[oc * 128:(oc + 1) * 128, :])

    pY_evac = [None, None]
    xr_ev = [None, None]
    x2st_ev = [None, None]
    zs_used = [None, None]
    otld_ev = [None] * 4
    otmm_done = [None] * 4
    for tl in range(NT):
        rg = tl % 4
        if otmm_done[rg] is not None:
            W("g", "st", otmm_done[rg])
        W("g", "dg", ot_store_done[tl])
        otld_ev[rg] = DMA("g", OTld[rg].ap().rearrange("p (oc q) -> p oc q", oc=NH),
                          OT_s.ap()[tl].rearrange("oc p q -> p oc q"))
        pg = tl % 2
        if pY_evac[pg] is not None:
            W("a", "sv", pY_evac[pg])
        xr_ev[pg] = DMA("a", xrows[pg][:], xin[(2 * tl) * 128:(2 * tl + 1) * 128, :])
        if tl == 0:
            W("t", "sa", p3_done_a)
            W("t", "dg", wot_ev)
        if pY_evac[pg] is not None:
            W("t", "sv", pY_evac[pg])
        W("t", "dg", otld_ev[rg])
        lastv = None
        for qn in range(4):
            for oc in range(NH):
                lastv = MM(pY[pg][:, qn * 512:(qn + 1) * 512],
                           OTld[rg][:, oc * 128:(oc + 1) * 128],
                           wot_sb[:, oc * 2048 + qn * 512: oc * 2048 + (qn + 1) * 512],
                           start=(oc == 0), stop=(oc == NH - 1),
                           sem="st" if (oc == NH - 1 and qn == 3) else None)
        otmm_done[rg] = lastv
        W("v", "st", lastv)
        W("v", "dx", xr_ev[pg])
        if zs_used[pg] is not None:
            W("v", "sa", zs_used[pg])
        zev = OP("v", lambda En, o=zstage[pg][:], i0=pY[pg][:], i1=xrows[pg][:],
                 ac=small[:, 3:4]:
                 En.scalar_tensor_tensor(o, i0, 1.0, i1, ALU.mult, ALU.add, accum_out=ac), "sv")
        pY_evac[pg] = zev
        W("a", "sv", zev)
        sq = OP("a", lambda En, o=sqstage[:], i=zstage[pg][:], ac=small[:, 4:5]:
                En.activation(o, i, AF.Square, accum_out=ac), "sa")
        zs_used[pg] = sq
        W("v", "sa", sq)
        f1 = OP("v", lambda En: En.tensor_scalar_mul(small[:, 5:6], small[:, 3:4], 1.0 / D), "sv")
        W("v", "sv", f1)
        f2 = OP("v", lambda En: En.tensor_tensor(small[:, 7:8], small[:, 5:6], small[:, 5:6], ALU.mult), "sv")
        W("v", "sv", f2)
        vv = OP("v", lambda En: En.scalar_tensor_tensor(small[:, 6:7], small[:, 4:5], 1.0 / D,
                                                        small[:, 7:8], ALU.mult, ALU.subtract), "sv")
        W("a", "sv", vv)
        f3 = OP("a", lambda En: En.activation(small[:, 9:10], small[:, 6:7], AF.Ln, bias=small[:, 10:11]), "sa")
        W("a", "sa", f3)
        l2 = OP("a", lambda En: En.activation(small[:, 8:9], small[:, 9:10], AF.Exp, scale=-0.5), "sa")
        W("v", "sa", l2)
        if x2st_ev[pg] is not None:
            W("v", "dx", x2st_ev[pg])
        nv = OP("v", lambda En, o=x2out[pg][:], i=zstage[pg][:]:
                En.tensor_scalar(o, i, small[:, 5:6], small[:, 8:9], ALU.subtract, ALU.mult), "sv")
        W("a", "sv", nv)
        x2st_ev[pg] = DMA("a", x2_s[tl * 128:(tl + 1) * 128, :], x2out[pg][:])

    p41_done_dx = C["dx"]
    p41_last_mm = otmm_done[(NT - 1) % 4]
    p41.close()
    ot_stack.close()

    p42 = contextlib.ExitStack()
    histage = p42.enter_context(nc.sbuf_tensor("histage", [128, D], BF16))
    lostage = p42.enter_context(nc.sbuf_tensor("lostage", [128, D], BF16))
    x2Tst = [p42.enter_context(nc.sbuf_tensor(f"x2Tst{i}", [128, D], BF16)) for i in range(2)]
    x2TLst = [p42.enter_context(nc.sbuf_tensor(f"x2TLst{i}", [128, D], BF16)) for i in range(2)]
    ghi = p42.enter_context(nc.sbuf_tensor("ghi", [128, NDC * E], BF16))
    glo = p42.enter_context(nc.sbuf_tensor("glo", [128, NDC * E], BF16))
    sct2 = p42.enter_context(nc.sbuf_tensor("sct2", [128, 16], F32))
    gw_sb = p42.enter_context(nc.sbuf_tensor("gw_sb", [128, NDC * E], F32))
    x2stage = [p42.enter_context(nc.sbuf_tensor(f"x2stage{i}", [128, D], F32)) for i in range(2)]
    p42.enter_context(nc.sbuf_tensor("p42pad", [128, 12096], mybir.dt.uint8))
    ptb2 = [p42.enter_context(nc.psum_tensor(f"ptb2{i}", [128, 1024], BF16)) for i in range(2)]
    psc = p42.enter_context(nc.psum_tensor("psc", [128, 512], F32))

    W("g", "st", p41_last_mm)
    W("g", "dx", p41_done_dx)
    gw_ev = DMA("g", gw_sb.ap().rearrange("p (dc e) -> p dc e", dc=NDC),
                gwt.ap().rearrange("(dc p) e -> p dc e", p=128))
    W("a", "dg", gw_ev)
    g1 = OP("a", lambda En: En.copy(ghi[:], gw_sb[:]), "sa")
    W("v", "sa", g1)
    OP("v", lambda En: En.tensor_tensor(glo[:], gw_sb[:], ghi[:], ALU.subtract), "sv")

    x2l_ev = [None, None]
    x2free = [None, None]
    hi_tr_last = None
    lo_tr_last = None
    tr2_evac = [None, None]
    x2T_store = [None, None]
    sg_prev = None
    top2_prev = None
    stage_free = [None, None]
    for tl in range(NT):
        rg = tl % 2
        if stage_free[rg] is not None:
            W("g", "st", stage_free[rg])
        if x2free[rg] is not None:
            W("g", "sv", x2free[rg])
        if tl < 2:
            W("g", "dx", p41_done_dx)
        x2l_ev[rg] = DMA("g", x2stage[rg][:], x2_s[tl * 128:(tl + 1) * 128, :])
        W("a", "dg", x2l_ev[rg])
        if hi_tr_last is not None:
            W("a", "st", hi_tr_last)
        hi_ev = OP("a", lambda En, o=histage[:], i=x2stage[rg][:]: En.copy(o, i), "sa")
        W("v", "sa", hi_ev)
        if lo_tr_last is not None:
            W("v", "st", lo_tr_last)
        lo_ev = OP("v", lambda En, o=lostage[:], i0=x2stage[rg][:], i1=histage[:]:
                   En.tensor_tensor(o, i0, i1, ALU.subtract), "sv")
        x2free[rg] = lo_ev
        hi_done = lo_done = None
        for (src, dst, evdep, which) in ((histage, x2Tst[rg], ("sa", hi_ev), 0),
                                         (lostage, x2TLst[rg], ("sv", lo_ev), 1)):
            W("t", evdep[0], evdep[1])
            for g in range(4):
                prg = g % 2
                if tr2_evac[prg] is not None:
                    W("t", tr2_evac[prg][0], tr2_evac[prg][1])
                lastv = None
                for c4 in range(4):
                    c = g * 4 + c4
                    lastv = TR(ptb2[prg][:, c4 * 128:(c4 + 1) * 128],
                               src[:, c * 128:(c + 1) * 128], idb_sb[:],
                               sem="st" if c4 == 3 else None)
                if g == 3:
                    if which == 0:
                        hi_tr_last = lastv
                    else:
                        lo_tr_last = lastv
                qe = "v" if (g % 2 == 0) else "a"
                se = "sv" if qe == "v" else "sa"
                W(qe, "st", lastv)
                if which == 0 and x2T_store[rg] is not None:
                    W(qe, "dg", x2T_store[rg])
                evv = OP(qe, lambda En, o=dst[:, g * 512:(g + 1) * 512], i=ptb2[prg][:, 0:512]:
                         CP(En, o, i), se)
                tr2_evac[prg] = (se, evv)
                if which == 0 and g == 3:
                    hi_done = (se, evv)
                if which == 1 and g == 3:
                    lo_done = (se, evv)
        W("g", hi_done[0], hi_done[1])
        x2T_store[rg] = DMA("g", x2T_s.ap()[:, tl * 128:(tl + 1) * 128].rearrange("(dc p) c -> p dc c", p=128),
                            x2Tst[rg].ap().rearrange("p (dc c) -> p dc c", dc=NDC))
        W("t", hi_done[0], hi_done[1])
        W("t", lo_done[0], lo_done[1])
        if sg_prev is not None:
            W("t", "sa", sg_prev)
        nterm = 0
        lastv = None
        for (xsb, gsb) in ((x2Tst[rg], ghi), (x2Tst[rg], glo), (x2TLst[rg], ghi)):
            for dc in range(NDC):
                lastv = MM(psc[:, 0:16], xsb[:, dc * 128:(dc + 1) * 128],
                           gsb[:, dc * E:(dc + 1) * E],
                           start=(nterm == 0), stop=(nterm == 3 * NDC - 1),
                           sem="st" if nterm == 3 * NDC - 1 else None)
                nterm += 1
        stage_free[rg] = lastv
        W("a", "st", lastv)
        if top2_prev is not None:
            W("a", "sv", top2_prev)
        sg = OP("a", lambda En: En.activation(sct2[:], psc[:, 0:16], AF.Sigmoid), "sa")
        sg_prev = sg
        W("v", "sa", sg)
        m1 = small[:, 16:17]; m2 = small[:, 17:18]; s12 = small[:, 18:19]; ri = small[:, 19:20]
        w1n = small[:, 20:21]; w2n = small[:, 21:22]
        eq1 = small[:, 32:48]; msk = small[:, 48:64]

        def FOP(fn):
            ev = OP("v", fn, "sv")
            W("v", "sv", ev)
            return ev
        FOP(lambda En: En.reduce_max(m1, sct2[:], axis=AX.X))
        FOP(lambda En: En.tensor_scalar(eq1, sct2[:], m1, None, ALU.is_equal))
        FOP(lambda En: En.scalar_tensor_tensor(msk, eq1, NEG, sct2[:], ALU.mult, ALU.add))
        FOP(lambda En: En.reduce_max(m2, msk, axis=AX.X))
        FOP(lambda En: En.tensor_tensor(s12, m1, m2, ALU.add))
        FOP(lambda En: En.reciprocal(ri, s12))
        FOP(lambda En: En.tensor_tensor(w1n, m1, ri, ALU.mult))
        FOP(lambda En: En.tensor_tensor(w2n, m2, ri, ALU.mult))
        FOP(lambda En: En.tensor_scalar(msk, msk, m2, None, ALU.is_equal))
        FOP(lambda En, o=dw_sb[:, tl * E:(tl + 1) * E]: En.tensor_scalar(o, eq1, w1n, None, ALU.mult))
        top2_prev = OP("v", lambda En, o=dw_sb[:, tl * E:(tl + 1) * E]:
                       En.scalar_tensor_tensor(o, msk, w2n, o, ALU.mult, ALU.add), "sv")

    p42_done_dg = C["dg"]
    p42.close()

    p5 = contextlib.ExitStack()
    silu_sb = [p5.enter_context(nc.sbuf_tensor(f"silu{i}", [128, GT], F32)) for i in range(2)]
    x2r = [p5.enter_context(nc.sbuf_tensor(f"x2r{i}", [128, D], F32)) for i in range(2)]
    yacc = p5.enter_context(nc.sbuf_tensor("yacc", [128, 8 * D], F32))
    x2Tg = p5.enter_context(nc.sbuf_tensor("x2Tg", [128, NDC * GT], BF16))
    hT = p5.enter_context(nc.sbuf_tensor("hT", [128, 6 * GT], BF16))
    w13b = [p5.enter_context(nc.sbuf_tensor(f"w13b{i}", [128, 2 * NDC * 128], BF16)) for i in range(2)]
    w2b = p5.enter_context(nc.sbuf_tensor("w2b", [128, 6 * D], BF16))
    outst = [p5.enter_context(nc.sbuf_tensor(f"outst{i}", [128, D], F32)) for i in range(2)]
    pR = [p5.enter_context(nc.psum_tensor(f"pR{i}", [128, 2048], F32)) for i in range(2)]

    units = [("s", None, 0, "init"), ("s", None, 11, "add")]
    units += [("e", e, 0, "dw") for e in range(E)]

    pR_read = [None, None]
    w13_use = [None, None]
    w13_ld = [None, None]
    w2_use = None
    w2_ld = [None] * 6
    hT_use = [None] * 6
    silu_use = [None, None]
    out_store = [None, None]
    x2r_used = [None, None]
    last_ab_grp = None
    strip_ctr = 0

    def w13_src(kind, e, i):
        if kind == "s":
            return sw13.ap()[i, :, :, :]
        return w13.ap()[e, i, :, :, :]

    def w2_src(kind, e, i):
        if kind == "s":
            return sw2r.ap()[i * 128:(i + 1) * 128, :]
        return w2r.ap()[e, i * 128:(i + 1) * 128, :]

    W("g", "dx", p41_done_dx)
    W("t", "sv", top2_prev)
    W("t", "sa", sg_prev)
    W("s", "sv", C["sv"])
    W("s", "sa", C["sa"])
    W("s", "st", C["st"])
    W("s", "dg", C["dg"])
    for grp in range(2):
        W("g", "dg", p42_done_dg)
        if last_ab_grp is not None:
            W("g", "st", last_ab_grp)
        gx_ev = DMA("g", x2Tg.ap().rearrange("p (dc c) -> p dc c", dc=NDC),
                    x2T_s.ap()[:, grp * GT:(grp + 1) * GT].rearrange("(dc p) c -> p dc c", p=128))
        first_unit = True
        for (ukind, ue, ubase, uacc) in units:
            for half, (hbase, hn) in enumerate(((0, 6), (6, 5))):
                for ii in range(hn):
                    i = ubase + hbase + ii
                    if w2_use is not None:
                        W("s", "st", w2_use)
                        w2_use = None
                    w2_ld[ii] = DMA("s", w2b[:, ii * D:(ii + 1) * D], w2_src(ukind, ue, i))
                for ii in range(hn):
                    i = ubase + hbase + ii
                    sl = strip_ctr % 2
                    if w13_use[sl] is not None:
                        W("s", "st", w13_use[sl])
                    w13_ld[sl] = DMA("s", w13b[sl].ap().rearrange("p (m dc c) -> p m dc c", m=2, dc=NDC),
                                     w13_src(ukind, ue, i).rearrange("m (dc p) c -> p m dc c", p=128))
                    rg = strip_ctr % 2
                    if pR_read[rg] is not None:
                        W("t", pR_read[rg][0], pR_read[rg][1])
                    W("t", "ds", w13_ld[sl])
                    if first_unit and ii == 0 and half == 0:
                        W("t", "dg", gx_ev)
                    lastv = None
                    for m in range(2):
                        for dc in range(NDC):
                            for qn in range(2):
                                lastv = MM(pR[rg][:, m * 1024 + qn * 512: m * 1024 + (qn + 1) * 512],
                                           w13b[sl][:, (m * NDC + dc) * 128:(m * NDC + dc + 1) * 128],
                                           x2Tg[:, dc * GT + qn * 512: dc * GT + (qn + 1) * 512],
                                           start=(dc == 0), stop=(dc == NDC - 1),
                                           sem="st" if (m == 1 and dc == NDC - 1 and qn == 1) else None)
                    w13_use[sl] = lastv
                    last_ab_grp = lastv
                    ssl = strip_ctr % 2
                    W("a", "st", lastv)
                    if silu_use[ssl] is not None:
                        W("a", "sv", silu_use[ssl])
                    sv_ = OP("a", lambda En, o=silu_sb[ssl][:], i=pR[rg][:, 0:1024]:
                             En.activation(o, i, AF.Silu), "sa")
                    W("v", "sa", sv_)
                    if hT_use[ii] is not None:
                        W("v", "st", hT_use[ii])
                    hm = OP("v", lambda En, o=hT[:, ii * GT:(ii + 1) * GT], i0=silu_sb[ssl][:],
                            i1=pR[rg][:, 1024:2048]: En.tensor_tensor(o, i0, i1, ALU.mult), "sv")
                    silu_use[ssl] = hm
                    pR_read[rg] = ("sv", hm)
                    strip_ctr += 1
                hm_last = hm
                first_unit = False
                for tt in range(8):
                    rg = tt % 2
                    if pR_read[rg] is not None:
                        W("t", pR_read[rg][0], pR_read[rg][1])
                    W("t", "sv", hm_last)
                    W("t", "ds", w2_ld[hn - 1])
                    lastv = None
                    for ii in range(hn):
                        for qn in range(4):
                            lastv = MM(pR[rg][:, qn * 512:(qn + 1) * 512],
                                       hT[:, ii * GT + tt * 128: ii * GT + (tt + 1) * 128],
                                       w2b[:, ii * D + qn * 512: ii * D + (qn + 1) * 512],
                                       start=(ii == 0), stop=(ii == hn - 1),
                                       sem="st" if (ii == hn - 1 and qn == 3) else None)
                        if tt == 7:
                            hT_use[ii] = lastv
                    if tt == 7:
                        w2_use = lastv
                    W("v", "st", lastv)
                    tl = grp * 8 + tt
                    ysl = yacc[:, tt * D:(tt + 1) * D]
                    if uacc == "init" and half == 0:
                        yev = OP("v", lambda En, o=ysl, i=pR[rg][:]: En.tensor_copy(o, i), "sv")
                    elif uacc in ("init", "add"):
                        yev = OP("v", lambda En, o=ysl, i=pR[rg][:]:
                                 En.scalar_tensor_tensor(o, i, 1.0, o, ALU.mult, ALU.add), "sv")
                    else:
                        dcol = dw_sb[:, tl * E + ue: tl * E + ue + 1]
                        yev = OP("v", lambda En, o=ysl, i=pR[rg][:], s=dcol:
                                 En.scalar_tensor_tensor(o, i, s, o, ALU.mult, ALU.add), "sv")
                    pR_read[rg] = ("sv", yev)
        for tt in range(8):
            rg = tt % 2
            tl = grp * 8 + tt
            if x2r_used[rg] is not None:
                W("g", x2r_used[rg][0], x2r_used[rg][1])
            xe = DMA("g", x2r[rg][:], x2_s[tl * 128:(tl + 1) * 128, :])
            W("v", "dg", xe)
            ysl = yacc[:, tt * D:(tt + 1) * D]
            z2 = OP("v", lambda En, o=ysl, i=x2r[rg][:]:
                    En.scalar_tensor_tensor(o, o, 1.0, i, ALU.mult, ALU.add,
                                            accum_out=small[:, 3:4]), "sv")
            W("a", "sv", z2)
            sq = OP("a", lambda En, o=x2r[rg][:], i=ysl, ac=small[:, 4:5]:
                    En.activation(o, i, AF.Square, accum_out=ac), "sa")
            x2r_used[rg] = ("sa", sq)
            W("v", "sa", sq)
            f1 = OP("v", lambda En: En.tensor_scalar_mul(small[:, 5:6], small[:, 3:4], 1.0 / D), "sv")
            W("v", "sv", f1)
            f2 = OP("v", lambda En: En.tensor_tensor(small[:, 7:8], small[:, 5:6], small[:, 5:6], ALU.mult), "sv")
            W("v", "sv", f2)
            vv = OP("v", lambda En: En.scalar_tensor_tensor(small[:, 6:7], small[:, 4:5], 1.0 / D,
                                                            small[:, 7:8], ALU.mult, ALU.subtract), "sv")
            W("a", "sv", vv)
            f3 = OP("a", lambda En: En.activation(small[:, 9:10], small[:, 6:7], AF.Ln, bias=small[:, 10:11]), "sa")
            W("a", "sa", f3)
            l2 = OP("a", lambda En: En.activation(small[:, 8:9], small[:, 9:10], AF.Exp, scale=-0.5), "sa")
            W("v", "sa", l2)
            if out_store[rg] is not None:
                W("v", "do", out_store[rg])
            nv = OP("v", lambda En, o=outst[rg][:], i=ysl:
                    En.tensor_scalar(o, i, small[:, 5:6], small[:, 8:9], ALU.subtract, ALU.mult), "sv")
            W("s", "sv", nv)
            out_store[rg] = DMAo(out[tl * 128:(tl + 1) * 128, :], outst[rg][:])

    Q["s"].append(("w", "do", C["do"]))
    p5.close()

    with (
        nc.semaphore("st") as h_st, nc.semaphore("sv") as h_sv,
        nc.semaphore("sa") as h_sa, nc.semaphore("sp") as h_sp,
        nc.semaphore("dg") as h_dg, nc.semaphore("ds") as h_ds,
        nc.semaphore("dx") as h_dx, nc.semaphore("do") as h_do,
        nc.Block() as block,
    ):
        SEM = {"st": h_st, "sv": h_sv, "sa": h_sa, "sp": h_sp,
               "dg": h_dg, "ds": h_ds, "dx": h_dx, "do": h_do}

        def runq(En, key):
            for item in Q[key]:
                if item[0] == "w":
                    En.wait_ge(SEM[item[1]], item[2])
                else:
                    ins = item[1](En)
                    if item[2] is not None:
                        ins.then_inc(SEM[item[2]], item[3])

        block.tensor(lambda En: runq(En, "t"))
        block.vector(lambda En: runq(En, "v"))
        block.scalar(lambda En: runq(En, "a"))
        block.gpsimd(lambda En: runq(En, "g"))
        block.sync(lambda En: runq(En, "s"))

    outer.close()
    return nc


def _prep(inputs):
    bf = ml_dtypes.bfloat16
    Wq = np.asarray(inputs["L0_attn_Wq"], np.float32)
    Wk = np.asarray(inputs["L0_attn_Wk"], np.float32)
    Wv = np.asarray(inputs["L0_attn_Wv"], np.float32)
    Wo = np.asarray(inputs["L0_attn_Wo"], np.float32)
    gw = np.asarray(inputs["L0_ffn_gate_w"], np.float32)
    w1 = np.asarray(inputs["L0_ffn_w1"])
    w2 = np.asarray(inputs["L0_ffn_w2"])
    w3 = np.asarray(inputs["L0_ffn_w3"])
    sw1 = np.asarray(inputs["L0_ffn_sw1"])
    sw2 = np.asarray(inputs["L0_ffn_sw2"])
    sw3 = np.asarray(inputs["L0_ffn_sw3"])

    def slabify(w):
        wt = np.ascontiguousarray(np.swapaxes(w, -1, -2))
        nI = wt.shape[-1] // 128
        s = wt.reshape(wt.shape[:-2] + (D, nI, 128))
        s = np.moveaxis(s, -2, -3)
        return np.ascontiguousarray(s)

    w1s = slabify(w1)
    w3s = slabify(w3)
    w13 = np.ascontiguousarray(np.stack([w1s, w3s], axis=2))
    sw1s = slabify(sw1)
    sw3s = slabify(sw3)
    sw13 = np.ascontiguousarray(np.stack([sw1s, sw3s], axis=1))
    w2r = np.ascontiguousarray(np.swapaxes(w2, -1, -2))
    sw2r = np.ascontiguousarray(sw2.T)

    return {
        "wqt": np.ascontiguousarray(Wq.T), "wkt": np.ascontiguousarray(Wk.T),
        "wvt": np.ascontiguousarray(Wv.T), "wot": np.ascontiguousarray(Wo.T),
        "gwt": np.ascontiguousarray(gw.T),
        "w13": w13, "w2r": w2r, "sw13": sw13, "sw2r": sw2r,
        "idf": np.eye(128, dtype=np.float32),
        "idb": np.eye(128, dtype=bf),
    }


def _core_maps(inputs, base):
    x = np.asarray(inputs["L0_attn_x"], np.float32)
    maps = []
    for c in range(8):
        b, j = c // 2, c % 2
        perm = np.empty(32, np.int64)
        for t in range(16):
            perm[2 * t] = 2 * t + j
            perm[2 * t + 1] = 2 * t + (1 - j)
        xb = np.ascontiguousarray(x[b].reshape(32, 128, D)[perm].reshape(T, D))
        kidx = (perm[:, None] * 128 + np.arange(128)[None, :]).reshape(T).astype(np.float32)
        iota = np.ascontiguousarray(np.broadcast_to(kidx, (128, T)))
        maskd = np.zeros((2, 128, 512), np.float32)
        qq = np.arange(128)[:, None]
        for par in range(2):
            t = par
            g_q = 2 * t + j
            cs = 4 * (t // 2)
            gk = (perm[cs:cs + 4, None] * 128 + np.arange(128)[None, :]).reshape(512)
            maskd[par] = np.where(gk[None, :] > (g_q * 128 + qq), np.float32(NEG), np.float32(0.0))
        m = dict(base)
        m["xb"] = xb
        m["iota"] = iota
        m["maskd"] = maskd
        maps.append(m)
    return maps


def kernel(**inputs):
    base = _prep(inputs)
    maps = _core_maps(inputs, base)
    nc = build()
    res = run_bass_kernel_spmd(nc, maps, core_ids=list(range(8)))
    outf = np.empty((B, T, D), np.float32)
    for c in range(8):
        b, j = c // 2, c % 2
        o = np.asarray(res.results[c]["out"])
        for t in range(16):
            g = 2 * t + j
            outf[b, g * 128:(g + 1) * 128, :] = o[t * 128:(t + 1) * 128, :]
    return outf
